# version 08
# baseline (speedup 1.0000x reference)
import contextlib
import numpy as np
import ml_dtypes
import concourse.bass as bass
import concourse.mybir as mybir
from concourse.bass_utils import run_bass_kernel_spmd

F32 = mybir.dt.float32
BF16 = mybir.dt.bfloat16
AF = mybir.ActivationFunctionType
ALU = mybir.AluOpType
AX = mybir.AxisListType

NCORES = 8
D = 2048
I = 4096
T = 4096
KC = 16
IC = 32
TL = 256
NT = T // TL
MAGIC = 12582912.0
C_D = float(1.0 / np.sqrt(np.float32(2048.0)))
C_I = 1.0 / 64.0
ENGS = ("sync", "gpsimd", "tensor", "vector", "scalar")


class Prog:
    def __init__(self, nc):
        self.nc = nc
        self.ops = {e: [] for e in ENGS}
        self.sems = {}
        self.cnt = {}
        self.es = contextlib.ExitStack()

    def sem(self, name):
        if name not in self.sems:
            self.sems[name] = self.es.enter_context(self.nc.semaphore(name))
            self.cnt[name] = 0
        return self.sems[name]

    def wait(self, eng, name, val=None):
        s = self.sem(name)
        v = self.cnt[name] if val is None else val
        if v > 0:
            self.ops[eng].append(lambda E, s=s, v=v: E.wait_ge(s, v))

    def dma(self, eng, sem, out, in_, transpose=False):
        s = self.sem(sem)
        self.cnt[sem] += 16
        if transpose:
            self.ops[eng].append(lambda E, o=out, i=in_, s=s: E.dma_start(
                out=o, in_=i, transpose=True).then_inc(s, 16))
        else:
            self.ops[eng].append(lambda E, o=out, i=in_, s=s: E.dma_start(
                out=o, in_=i).then_inc(s, 16))
        return self.cnt[sem]

    def op(self, eng, fn, sem=None, inc=1):
        if sem is None:
            self.ops[eng].append(fn)
            return None
        s = self.sem(sem)
        self.cnt[sem] += inc
        self.ops[eng].append(lambda E, f=fn, s=s, n=inc: f(E).then_inc(s, n))
        return self.cnt[sem]


def bfly(p, src3, dst3, nch, h):
    n = nch // (2 * h)
    sv = src3.rearrange("p (n t h) w -> p n t h w", t=2, h=h)
    dv = dst3.rearrange("p (n t h) w -> p n t h w", t=2, h=h)
    p.op("vector", lambda E: E.tensor_tensor(
        dv[:, :, 0], sv[:, :, 0], sv[:, :, 1], ALU.add), sem="vec")
    p.op("vector", lambda E: E.tensor_tensor(
        dv[:, :, 1], sv[:, :, 0], sv[:, :, 1], ALU.subtract), sem="vec")


def bc3(ap2, n):
    return ap2.rearrange("p (o w) -> p o w", o=1).broadcast_to(
        (128, n, ap2.shape[-1]))


def bcc(ap2, n):
    return ap2.rearrange("p (c w) -> p c w", w=1).broadcast_to(
        (128, ap2.shape[-1], n))


def build_nc(debug=False):
    nc = bass.Bass("TRN2", target_bir_lowering=False, debug=False)
    p = Prog(nc)
    es = p.es

    x_in = nc.dram_tensor("x", [D, T], F32, kind="ExternalInput").ap()
    W_in = nc.dram_tensor("W_in", [D, D], F32, kind="ExternalInput").ap()
    W_dec = nc.dram_tensor("W_dec", [D, D], F32, kind="ExternalInput").ap()
    W_gate = nc.dram_tensor("W_gate", [D, D], F32, kind="ExternalInput").ap()
    W_out = nc.dram_tensor("W_out", [D, D], F32, kind="ExternalInput").ap()
    w_upT = nc.dram_tensor("w_upT", [D, I], F32, kind="ExternalInput").ap()
    w_dnT = nc.dram_tensor("w_dnT", [I, D], F32, kind="ExternalInput").ap()
    b_in = nc.dram_tensor("b_in", [128, KC], F32, kind="ExternalInput").ap()
    b_dec = nc.dram_tensor("b_dec", [128, KC], F32, kind="ExternalInput").ap()
    b_dbias = nc.dram_tensor("b_dbias", [128, KC], F32, kind="ExternalInput").ap()
    b_gate = nc.dram_tensor("b_gate", [128, KC], F32, kind="ExternalInput").ap()
    b_out = nc.dram_tensor("b_out", [128, KC], F32, kind="ExternalInput").ap()
    H128_in = nc.dram_tensor("H128", [128, 128], BF16, kind="ExternalInput").ap()
    identb_in = nc.dram_tensor("identb", [128, 128], BF16, kind="ExternalInput").ap()
    identf_in = nc.dram_tensor("identf", [128, 128], F32, kind="ExternalInput").ap()
    sel_in = nc.dram_tensor("sel", [128, NCORES], F32, kind="ExternalInput").ap()

    y_out = nc.dram_tensor("y", [T, D], F32, kind="ExternalOutput").ap()

    def scratch(name, shape, dtype):
        kind = "ExternalOutput" if debug else "Internal"
        return nc.dram_tensor(name, shape, dtype, kind=kind).ap()

    xT_sp = scratch("xT_sp", [D, T], BF16)
    s0_sp = scratch("s0_sp", [D, T], BF16)
    cp_sp = scratch("cp_sp", [D, T], BF16)
    y_sp = scratch("y_sp", [D, T], BF16)
    xhat_sp = scratch("xhat_sp", [D, T], F32)
    xhatb_sp = scratch("xhatb_sp", [D, T], BF16)
    xq2_sp = scratch("xq2_sp", [I, T], BF16)
    fold_sp = scratch("fold_sp", [NT, 128, 2], F32)
    if debug:
        u_dbg = nc.dram_tensor("u_dbg", [D, T], F32, kind="ExternalOutput").ap()
        i_dbg = nc.dram_tensor("i_dbg", [D, T], BF16, kind="ExternalOutput").ap()
        dm_dbg = nc.dram_tensor("dm_dbg", [D, T], F32, kind="ExternalOutput").ap()
        u_dbg_v = u_dbg.rearrange("(c p) t -> p c t", p=128)
        init_dbg = nc.dram_tensor("init_dbg", [128, KC], F32, kind="ExternalOutput").ap()
        g_dbg = nc.dram_tensor("g_dbg", [D, TL], BF16, kind="ExternalOutput").ap()
        ag_dbg = nc.dram_tensor("ag_dbg", [128, NCORES, 2 * KC], F32, kind="ExternalOutput").ap()
        xtb_dbg = nc.dram_tensor("xtb_dbg", [128, KC, TL], BF16, kind="ExternalOutput").ap()
        wqd_dbg = nc.dram_tensor("wqd_dbg", [128, IC, D], BF16, kind="ExternalOutput").ap()
        alpha_dbg = nc.dram_tensor("alpha_dbg", [128, 4], F32, kind="ExternalOutput").ap()
        i_dbg_v = i_dbg.rearrange("(c p) t -> p c t", p=128)
        dm_dbg_v = dm_dbg.rearrange("(c p) t -> p c t", p=128)
    if debug:
        xh1_dbg = nc.dram_tensor("xh1_dbg", [D, T], BF16, kind="ExternalOutput").ap()
        tok_dbg = nc.dram_tensor("tok_dbg", [NT, 128, 6], F32, kind="ExternalOutput").ap()
        srow_dbg = nc.dram_tensor("srow_dbg", [NT, 1, 256], F32, kind="ExternalOutput").ap()
        am_dbg = nc.dram_tensor("am_dbg", [NT, 128, 256], F32, kind="ExternalOutput").ap()
        xh1_dbg_v = xh1_dbg.rearrange("(c p) t -> p c t", p=128)
        q1_dbg = nc.dram_tensor("q1_dbg", [D, T], BF16, kind="ExternalOutput").ap()
        h_dbg = nc.dram_tensor("h_dbg", [I, T], BF16, kind="ExternalOutput").ap()
        xh2_dbg = nc.dram_tensor("xh2_dbg", [I, T], BF16, kind="ExternalOutput").ap()
        cs_dbg = nc.dram_tensor("cs_dbg", [NT, 128, 512], F32, kind="ExternalOutput").ap()
        q1_dbg_v = q1_dbg.rearrange("(c p) t -> p c t", p=128)
        h_dbg_v = h_dbg.rearrange("(c p) t -> p c t", p=128)
        xh2_dbg_v = xh2_dbg.rearrange("(c p) t -> p c t", p=128)
    cc_in = nc.dram_tensor("cc_in", [128, 2 * KC], F32, kind="Internal").ap()
    cc_out = nc.dram_tensor("cc_out", [NCORES, 128, 2 * KC], F32,
                            kind="Internal").ap()

    def fvw(ap2d):
        return ap2d.rearrange("(c p) t -> p c t", p=128)

    V = dict(xT=fvw(xT_sp), s0=fvw(s0_sp), cp=fvw(cp_sp), y=fvw(y_sp),
             xhat=fvw(xhat_sp), xhatb=fvw(xhatb_sp), xq2=fvw(xq2_sp))

    def sb(name, shape, dt):
        return es.enter_context(nc.sbuf_tensor(name, shape, dt))

    C = dict(
        bin=sb("c_bin", [128, KC], F32)[:],
        bdec=sb("c_bdec", [128, KC], F32)[:],
        bdecn=sb("c_bdecn", [128, KC], F32)[:],
        bgate=sb("c_bgate", [128, KC], F32)[:],
        bout=sb("c_bout", [128, KC], F32)[:],
        H=sb("c_H", [128, 128], BF16)[:],
        idb=sb("c_idb", [128, 128], BF16)[:],
        idf=sb("c_idf", [128, 128], F32)[:],
        sel=sb("c_sel", [128, NCORES], F32)[:],
        ones1=sb("c_ones1", [1, 128], F32)[:],
        onesc=sb("c_onesc", [128, 1], BF16)[:],
        onescf=sb("c_onescf", [128, 1], F32)[:],
        init=sb("c_init", [128, KC], F32)[:],
        alpha=sb("c_alpha", [128, 4], F32)[:],
        scar=sb("c_scar", [128, KC], F32)[:],
        pcar=sb("c_pcar", [128, KC], F32)[:],
    )
    arb_t = sb("arena_b", [128, 90112], BF16)
    arf_t = sb("arena_f", [128, 7168], F32)

    def ARB(off, chunks, width):
        v = arb_t[:, off:off + chunks * width]
        return v if chunks == 1 else v.rearrange("p (c w) -> p c w", c=chunks)

    def ARF(off, chunks, width):
        v = arf_t[:, off:off + chunks * width]
        return v if chunks == 1 else v.rearrange("p (c w) -> p c w", c=chunks)

    def ARFR(off, width, nparts=1):
        return arf_t[0:nparts, off:off + width]

    PBIG = es.enter_context(nc.psum_tensor("pbig", [128, 2048], F32))[:]
    PS2 = es.enter_context(nc.psum_tensor("ps2", [128, 1024], F32))[:]
    PS_ROW = es.enter_context(nc.psum_tensor("ps_row", [1, 1024], F32))[:]

    p.dma("sync", "ld0", C["bin"], b_in)
    p.dma("sync", "ld0", C["bdec"], b_dec)
    p.dma("sync", "ld0", C["bgate"], b_gate)
    p.dma("sync", "ld0", C["bout"], b_out)
    p.dma("sync", "ld0", C["H"], H128_in)
    p.dma("sync", "ld0", C["idb"], identb_in)
    p.dma("sync", "ld0", C["idf"], identf_in)
    p.dma("sync", "ld0", C["sel"], sel_in)
    p.dma("sync", "ld0", C["bdecn"], b_dbias)
    p.wait("vector", "ld0")
    p.op("vector", lambda E: E.tensor_tensor(C["bdec"], C["bdec"], C["bdecn"], ALU.add), sem="vec")
    p.op("vector", lambda E: E.memset(C["ones1"], 1.0), sem="vec")
    p.op("vector", lambda E: E.memset(C["onesc"], 1.0), sem="vec")
    p.op("vector", lambda E: E.memset(C["onescf"], 1.0), sem="vec")
    p.op("vector", lambda E: E.memset(C["scar"], 0.0), sem="vec")
    p.op("vector", lambda E: E.memset(C["pcar"], 1.0), sem="vec")

    p.dma("gpsimd", "gcast", xT_sp, x_in)
    p.wait("sync", "gcast")

    w_in_b = ARB(0, KC, D)
    w_dec_b = ARB(KC * D, KC, D)
    xt_a = [ARB(2 * KC * D + i * KC * 512, KC, 512) for i in range(2)]
    inp_a = [ARB(2 * KC * D + 2 * KC * 512 + i * 1024, 1, 1024) for i in range(2)]
    d_a = [ARF(i * 1024, 1, 1024) for i in range(2)]
    u_a = [ARF(2048 + i * 1024, 1, 1024) for i in range(2)]
    s0b = ARF(4096, 1, 1024)
    cpb = ARF(5120, 1, 1024)
    zeros_a = ARF(6144, 1, 1024)

    p.dma("gpsimd", "gw", w_in_b, W_in.rearrange("(k p) e -> p k e", p=128))
    p.dma("gpsimd", "gw", w_dec_b, W_dec.rearrange("(k p) e -> p k e", p=128))
    p.op("vector", lambda E: E.memset(zeros_a, 0.0), sem="vec")
    p.wait("tensor", "gw")

    free_bank = {}
    free_xt = [0, 0]
    free_du = {}
    L = 1024
    for s in range(4):
        for e in range(KC):
            par = e % 2
            d_t, u_t, i_t = d_a[par], u_a[par], inp_a[par]
            for tau in range(2):
                tile = s * 2 + tau
                ldsem = "ld%d" % (tile % 2)
                if e == 0:
                    if free_xt[tile % 2]:
                        p.wait("sync", "mm", free_xt[tile % 2])
                    p.dma("sync", ldsem, xt_a[tile % 2],
                          V["xT"][:, :, tile * 512:(tile + 1) * 512])
                    p.wait("tensor", ldsem)
                bk_in = PBIG[:, (2 * tau) * 512:(2 * tau) * 512 + 512]
                bk_d = PBIG[:, (2 * tau + 1) * 512:(2 * tau + 1) * 512 + 512]
                for slot in (2 * tau, 2 * tau + 1):
                    if slot in free_bank:
                        p.wait("tensor", "scl", free_bank[slot])
                xtile = xt_a[tile % 2]
                for k in range(KC):
                    p.op("tensor", lambda E, k=k, e=e, x=xtile, bk=bk_in: E.matmul(
                        bk, w_in_b[:, k, e * 128:(e + 1) * 128], x[:, k, :],
                        start=(k == 0), stop=(k == KC - 1)))
                for k in range(KC):
                    last = k == KC - 1
                    p.op("tensor", lambda E, k=k, e=e, x=xtile, bk=bk_d: E.matmul(
                        bk, w_dec_b[:, k, e * 128:(e + 1) * 128], x[:, k, :],
                        start=(k == 0), stop=last), sem="mm" if last else None)
                if e == KC - 1:
                    free_xt[tile % 2] = p.cnt["mm"]
                p.wait("scalar", "mm", p.cnt["mm"])
                if tau == 0 and par in free_du:
                    p.wait("scalar", "vec", free_du[par])
                    if debug:
                        p.wait("scalar", "gsp")
                sl = slice(tau * 512, tau * 512 + 512)
                p.op("scalar", lambda E, b=bk_in, o=i_t, sl=sl, e=e: E.activation(
                    o[:, sl], b, AF.Silu, bias=C["bin"][:, e:e + 1], scale=np.float32(1.0)), sem="scl")
                p.op("scalar", lambda E, b=bk_d, o=d_t, sl=sl, e=e: E.activation(
                    o[:, sl], b, AF.Sigmoid, bias=C["bdec"][:, e:e + 1], scale=np.float32(1.0)), sem="scl")
                free_bank[2 * tau] = p.cnt["scl"]
                free_bank[2 * tau + 1] = p.cnt["scl"]
            p.wait("vector", "scl", p.cnt["scl"])
            p.op("vector", lambda E, u=u_t, d=d_t, i=i_t: E.tensor_tensor(
                u, d, i, ALU.mult), sem="vec")
            p.op("vector", lambda E, u=u_t, i=i_t: E.tensor_tensor(
                u, i, u, ALU.subtract), sem="vec")
            if (s, e) != (0, 0):
                p.wait("vector", "gsp")
            p.op("vector", lambda E, d=d_t, u=u_t, e=e: E.tensor_tensor_scan(
                s0b, d, u, C["scar"][:, e:e + 1], ALU.mult, ALU.add), sem="vec")
            p.op("vector", lambda E, d=d_t, e=e: E.tensor_tensor_scan(
                cpb, d, zeros_a, C["pcar"][:, e:e + 1], ALU.mult, ALU.add), sem="vec")
            p.wait("vector", "vec", p.cnt["vec"])
            p.op("vector", lambda E, e=e: E.tensor_copy(
                C["scar"][:, e:e + 1], s0b[:, L - 1:L]), sem="vec")
            p.op("vector", lambda E, e=e: E.tensor_copy(
                C["pcar"][:, e:e + 1], cpb[:, L - 1:L]), sem="vec")
            free_du[par] = p.cnt["vec"]
            p.wait("gpsimd", "vec", p.cnt["vec"])
            p.dma("gpsimd", "gsp", V["s0"][:, e, s * L:(s + 1) * L], s0b)
            p.dma("gpsimd", "gsp", V["cp"][:, e, s * L:(s + 1) * L], cpb)
            if debug:
                p.dma("gpsimd", "gsp", u_dbg_v[:, e, s * L:(s + 1) * L], u_t)
                p.dma("gpsimd", "gsp", i_dbg_v[:, e, s * L:(s + 1) * L], i_t)
                p.dma("gpsimd", "gsp", dm_dbg_v[:, e, s * L:(s + 1) * L], d_t)
    p.wait("sync", "vec", p.cnt["vec"])
    p.dma("sync", "sp", cc_in[:, 0:KC], C["pcar"])
    p.dma("sync", "sp", cc_in[:, KC:2 * KC], C["scar"])

    ag = ARF(0, NCORES, 2 * KC)
    ccs = [ARF(256, 1, KC), ARF(256 + KC, 1, KC)]
    acs = [ARF(256 + 2 * KC, 1, KC), ARF(256 + 3 * KC, 1, KC)]
    tmpx = ARF(256 + 4 * KC, 1, KC)
    p.wait("gpsimd", "sp")
    p.op("gpsimd", lambda E: E.collective_compute(
        "AllGather", ALU.bypass, replica_groups=[list(range(NCORES))],
        ins=[cc_in], outs=[cc_out]), sem="cc")
    p.wait("sync", "cc", p.cnt["cc"])
    p.dma("sync", "ld0", ag, cc_out.rearrange("g p t -> p g t"))
    p.wait("vector", "ld0")
    p.op("vector", lambda E: E.memset(ccs[0], 0.0), sem="vec")
    p.op("vector", lambda E: E.memset(acs[0], 0.0), sem="vec")
    for j in range(NCORES):
        cj, cj1 = ccs[j % 2], ccs[(j + 1) % 2]
        aj, aj1 = acs[j % 2], acs[(j + 1) % 2]
        if j == 4:
            p.op("vector", lambda E, c=cj: E.memset(c, 0.0), sem="vec")
        p.wait("vector", "vec", p.cnt["vec"])
        p.op("vector", lambda E, c=cj, a=aj, o=aj1, j=j: E.scalar_tensor_tensor(
            o, c, C["sel"][:, j:j + 1], a, ALU.mult, ALU.add), sem="vec")
        p.wait("vector", "vec", p.cnt["vec"])
        p.op("vector", lambda E, c=cj, j=j: E.tensor_tensor(
            tmpx, ag[:, j, 0:KC], c, ALU.mult), sem="vec")
        p.wait("vector", "vec", p.cnt["vec"])
        p.op("vector", lambda E, o=cj1, j=j: E.tensor_tensor(
            o, tmpx, ag[:, j, KC:2 * KC], ALU.add), sem="vec")
        p.wait("vector", "vec", p.cnt["vec"])
    p.op("vector", lambda E, a=acs[NCORES % 2]: E.tensor_copy(C["init"], a), sem="vec")
    if debug:
        p.wait("sync", "vec", p.cnt["vec"])
        p.dma("sync", "sp", init_dbg, C["init"])
        p.dma("sync", "sp", ag_dbg, ag)

    wg = ARB(0, KC, D)
    o = KC * D
    xt_b = [ARB(o, KC, TL), ARB(o + KC * TL, KC, TL)]
    o += 2 * KC * TL
    s0_b = [ARB(o, KC, TL), ARB(o + KC * TL, KC, TL)]
    o += 2 * KC * TL
    cp_b = [ARB(o, KC, TL), ARB(o + KC * TL, KC, TL)]
    o += 2 * KC * TL
    gt_b = ARB(o, KC, TL)
    yt_b = ARB(o + KC * TL, KC, TL)
    tmp16 = ARF(0, KC, TL)

    p.wait("gpsimd", "mm", p.cnt["mm"])
    p.dma("gpsimd", "gw", wg, W_gate.rearrange("(k p) e -> p k e", p=128))
    p.wait("tensor", "gw")
    p.wait("sync", "gsp")
    free_bank = {}
    free_ld = [0, 0]
    free_g = 0
    ld_mark = [0] * (NT + 1)

    def b1_load(t):
        par = t % 2
        if free_ld[par]:
            p.wait("sync", "vec", free_ld[par])
        p.dma("sync", "ld%d" % par, xt_b[par], V["xT"][:, :, t * TL:(t + 1) * TL])
        p.dma("sync", "ld%d" % par, s0_b[par], V["s0"][:, :, t * TL:(t + 1) * TL])
        ld_mark[t] = p.dma("sync", "ld%d" % par, cp_b[par],
                           V["cp"][:, :, t * TL:(t + 1) * TL])

    b1_load(0)
    for t in range(NT):
        par = t % 2
        ldsem = "ld%d" % par
        p.wait("tensor", ldsem, ld_mark[t])
        for e in range(KC):
            slot = e % 4
            bk = PBIG[:, slot * 512:slot * 512 + TL]
            if slot in free_bank:
                p.wait("tensor", "scl", free_bank[slot])
            for k in range(KC):
                last = k == KC - 1
                p.op("tensor", lambda E, k=k, e=e, bk=bk, x=xt_b[par]: E.matmul(
                    bk, wg[:, k, e * 128:(e + 1) * 128], x[:, k, :],
                    start=(k == 0), stop=last), sem="mm" if last else None)
            p.wait("scalar", "mm", p.cnt["mm"])
            if free_g:
                p.wait("scalar", "vec", free_g)
                if debug:
                    p.wait("scalar", "sp")
            p.op("scalar", lambda E, bk=bk, e=e: E.activation(
                gt_b[:, e, :], bk, AF.Sigmoid, bias=C["bgate"][:, e:e + 1],
                scale=np.float32(1.0)), sem="scl")
            free_bank[slot] = p.cnt["scl"]
        if t + 1 < NT:
            b1_load(t + 1)
        p.wait("vector", "scl", p.cnt["scl"])
        p.wait("vector", ldsem, ld_mark[t])
        p.wait("vector", "sp")
        for e in range(KC):
            p.op("vector", lambda E, e=e, par=par: E.scalar_tensor_tensor(
                tmp16[:, e, :], cp_b[par][:, e, :], C["init"][:, e:e + 1],
                s0_b[par][:, e, :], ALU.mult, ALU.add), sem="vec")
        p.wait("vector", "vec", p.cnt["vec"])
        p.op("vector", lambda E: E.tensor_tensor(
            yt_b[:], tmp16[:], gt_b[:], ALU.mult), sem="vec")
        free_g = p.cnt["vec"]
        free_ld[par] = p.cnt["vec"]
        p.wait("sync", "vec", p.cnt["vec"])
        p.dma("sync", "sp", V["y"][:, :, t * TL:(t + 1) * TL], yt_b)
        if debug and t == 0:
            p.dma("sync", "sp", g_dbg.rearrange("(c p) t -> p c t", p=128), gt_b)

    wo = ARB(0, KC, D)
    o = KC * D
    yt2 = [ARB(o, KC, TL), ARB(o + KC * TL, KC, TL)]
    o += 2 * KC * TL
    xt2 = [ARB(o, KC, TL), ARB(o + KC * TL, KC, TL)]
    o += 2 * KC * TL
    rb2 = ARB(o, KC, TL)
    r2b2 = ARB(o + KC * TL, KC, TL)
    rr = ARF(0, KC, TL)
    rows = ARFR(KC * TL, 4 * TL)
    mrow = ARFR(KC * TL + 4 * TL, TL)
    bcb = ARF(KC * TL + 5 * TL, 1, 2 * TL)

    p.wait("gpsimd", "mm", p.cnt["mm"])
    p.dma("gpsimd", "gw", wo, W_out.rearrange("(k p) e -> p k e", p=128))
    p.wait("tensor", "gw")
    p.wait("sync", "sp")

    def ln_rows(rowbuf, mbuf, src_sum, src_sq):
        p.op("vector", lambda E: E.tensor_scalar(
            mbuf, src_sum, 1.0 / D, None, ALU.mult), sem="vec")
        p.wait("vector", "vec", p.cnt["vec"])
        p.op("vector", lambda E: E.tensor_tensor(
            rowbuf[:, 0:TL], mbuf, mbuf, ALU.mult), sem="vec")
        p.wait("vector", "vec", p.cnt["vec"])
        p.op("vector", lambda E: E.scalar_tensor_tensor(
            rowbuf[:, TL:2 * TL], src_sq, 1.0 / D, rowbuf[:, 0:TL],
            ALU.mult, ALU.subtract), sem="vec")
        p.wait("scalar", "vec", p.cnt["vec"])
        p.op("scalar", lambda E: E.activation(
            rowbuf[:, 2 * TL:3 * TL], rowbuf[:, TL:2 * TL], AF.Sqrt,
            bias=np.float32(1e-5), scale=np.float32(1.0)), sem="scl")
        p.wait("vector", "scl", p.cnt["scl"])
        p.op("vector", lambda E: E.reciprocal(
            rowbuf[:, 3 * TL:4 * TL], rowbuf[:, 2 * TL:3 * TL]), sem="vec")
        p.wait("vector", "vec", p.cnt["vec"])
        p.op("vector", lambda E: E.tensor_tensor(
            rowbuf[:, 0:TL], mbuf, rowbuf[:, 3 * TL:4 * TL], ALU.mult), sem="vec")

    free_reg = [0, 0]
    free_ld = [0, 0]
    free_rr = 0
    ld_mark2 = [0] * (NT + 1)

    def b2_load(t):
        par = t % 2
        if free_ld[par]:
            p.wait("sync", "mm", free_ld[par])
        p.dma("sync", "ld%d" % par, yt2[par], V["y"][:, :, t * TL:(t + 1) * TL])
        ld_mark2[t] = p.dma("sync", "ld%d" % par, xt2[par],
                            V["xT"][:, :, t * TL:(t + 1) * TL])

    b2_load(0)
    apply_mark = [0] * (NT + 1)
    spill_marks = [(0, 0)] * (NT + 1)
    stat_mark = [0] * (NT + 1)

    def b2_tail(t):
        if t >= 1:
            p.wait("scalar", "mm", stat_mark[t - 1])
        p.op("scalar", lambda E: E.copy(rb2[:], rr[:]), sem="scl")
        p.op("scalar", lambda E: E.square(r2b2[:], rr[:]), sem="scl")
        p.wait("tensor", "scl", p.cnt["scl"])
        for k in range(KC):
            p.op("tensor", lambda E, k=k: E.matmul(
                PS_ROW[:, 0:TL], C["onesc"], rb2[:, k, :],
                start=(k == 0), stop=(k == KC - 1)))
        for k in range(KC):
            last = k == KC - 1
            p.op("tensor", lambda E, k=k: E.matmul(
                PS_ROW[:, 512:512 + TL], C["onesc"], r2b2[:, k, :],
                start=(k == 0), stop=last), sem="mm" if last else None)
        stat_mark[t] = p.cnt["mm"]
        p.wait("vector", "mm", p.cnt["mm"])
        ln_rows(rows, mrow, PS_ROW[:, 0:TL], PS_ROW[:, 512:512 + TL])
        p.wait("tensor", "vec", p.cnt["vec"])
        p.op("tensor", lambda E: E.matmul(
            PS2[:, 0:TL], C["ones1"], rows[:, 3 * TL:4 * TL], start=True, stop=True))
        p.op("tensor", lambda E: E.matmul(
            PS2[:, 512:512 + TL], C["ones1"], rows[:, 0:TL], start=True, stop=True),
            sem="mm")
        p.wait("scalar", "mm", p.cnt["mm"])
        p.op("scalar", lambda E: E.copy(bcb[:, 0:TL], PS2[:, 0:TL]), sem="scl")
        p.op("scalar", lambda E: E.copy(bcb[:, TL:2 * TL], PS2[:, 512:512 + TL]), sem="scl")
        p.wait("vector", "scl", p.cnt["scl"])
        if t > 0:
            p.wait("vector", "sp")
            p.wait("vector", "gxh")
        p.op("vector", lambda E: E.tensor_tensor(
            rr[:], rr[:], bc3(bcb[:, 0:TL], KC), ALU.mult), sem="vec")
        p.op("vector", lambda E: E.tensor_tensor(
            rr[:], rr[:], bc3(bcb[:, TL:2 * TL], KC), ALU.subtract), sem="vec")
        apply_mark[t] = p.cnt["vec"]
        p.wait("sync", "vec", p.cnt["vec"])
        s1 = p.dma("sync", "sp", V["xhat"][:, :, t * TL:(t + 1) * TL], rr)
        p.wait("gpsimd", "vec", p.cnt["vec"])
        s2 = p.dma("gpsimd", "gxh", V["xhatb"][:, :, t * TL:(t + 1) * TL], rr)
        spill_marks[t] = (s1, s2)

    for t in range(NT):
        par = t % 2
        ldsem = "ld%d" % par
        p.wait("tensor", ldsem, ld_mark2[t])
        for g in range(4):
            reg = g % 2
            if free_reg[reg]:
                p.wait("tensor", "scl", free_reg[reg])
            for ei in range(4):
                e = g * 4 + ei
                bk = PBIG[:, reg * 1024 + ei * TL: reg * 1024 + (ei + 1) * TL]
                for k in range(KC):
                    p.op("tensor", lambda E, k=k, e=e, bk=bk, y=yt2[par]: E.matmul(
                        bk, wo[:, k, e * 128:(e + 1) * 128], y[:, k, :],
                        start=(k == 0), stop=False))
                p.op("tensor", lambda E, e=e, bk=bk, x=xt2[par]: E.matmul(
                    bk, C["idb"], x[:, e, :], start=False, stop=True),
                    sem="mm" if ei == 3 else None)
            mm_g = p.cnt["mm"]
            if g == 0 and t > 0:
                b2_tail(t - 1)
            p.wait("scalar", "mm", mm_g)
            if t > 0:
                p.wait("scalar", "vec", apply_mark[t - 1])
                p.wait("scalar", "sp")
                p.wait("scalar", "gxh")
            for ei in range(4):
                e = g * 4 + ei
                p.op("scalar", lambda E, g=g, reg=reg, ei=ei, e=e: E.activation(
                    rr[:, e, :],
                    PBIG[:, reg * 1024 + ei * TL: reg * 1024 + (ei + 1) * TL],
                    AF.Identity, bias=C["bout"][:, e:e + 1],
                    scale=np.float32(1.0)), sem="scl")
            free_reg[reg] = p.cnt["scl"]
        free_ld[par] = p.cnt["mm"]
        if t + 1 < NT:
            b2_load(t + 1)
    b2_tail(NT - 1)

    SW = 2048

    def ternary_prep(W, wq, nk, nm, col):
        nsub = nm // SW
        nkk = nk * nsub
        stage_f = [ARF(i * SW, 1, SW) for i in range(2)]
        stage_b = ARB(65536, 1, SW)
        srow = arf_t[0:1, 2 * SW + 1:2 * SW + 3]
        wv = W.rearrange("(k p) m -> p k m", p=128)
        accrow = arf_t[0:1, 2 * SW + 16:2 * SW + 16 + 512]
        last_use = [0, 0]
        nsl = SW // 512
        ldm = [0] * (nkk + 1)

        def prep_load(c):
            pr = c % 2
            if last_use[pr]:
                p.wait("sync", "mm", last_use[pr])
            k, sub = divmod(c, nsub)
            ldm[c] = p.dma("sync", "ld%d" % pr, stage_f[pr],
                           wv[:, k, sub * SW:(sub + 1) * SW])

        prep_load(0)
        for c in range(nkk):
            pr = c % 2
            p.wait("scalar", "ld%d" % pr, ldm[c])
            p.wait("scalar", "mm", p.cnt["mm"])
            p.op("scalar", lambda E, pr=pr: E.activation(
                stage_b, stage_f[pr], AF.Abs, bias=np.float32(0.0),
                scale=np.float32(1.0)), sem="scl")
            if c + 1 < nkk:
                prep_load(c + 1)
            p.wait("tensor", "scl", p.cnt["scl"])
            if c >= 1:
                p.wait("tensor", "vec", p.cnt["vec"])
            for s_ in range(nsl):
                p.op("tensor", lambda E, s_=s_: E.matmul(
                    PS_ROW[0:1, 0:512], C["onesc"],
                    stage_b[:, s_ * 512:(s_ + 1) * 512],
                    start=(s_ == 0), stop=(s_ == nsl - 1)),
                    sem="mm" if s_ == nsl - 1 else None)
            last_use[pr] = p.cnt["mm"]
            p.wait("vector", "mm", p.cnt["mm"])
            if c == 0:
                p.op("vector", lambda E: E.tensor_copy(
                    accrow, PS_ROW[0:1, 0:512]), sem="vec")
            else:
                p.op("vector", lambda E: E.tensor_tensor(
                    accrow, accrow, PS_ROW[0:1, 0:512], ALU.add), sem="vec")
            p.wait("vector", "vec", p.cnt["vec"])
        p.op("vector", lambda E: E.tensor_reduce(
            srow[:, 0:1], accrow, AX.X, ALU.add), sem="vec")
        p.wait("vector", "vec", p.cnt["vec"])
        p.op("vector", lambda E: E.tensor_scalar(
            srow[:, 0:1], srow[:, 0:1], 1.0 / (nk * 128 * nm), None,
            ALU.mult), sem="vec")
        p.wait("vector", "scl", p.cnt["scl"])
        p.op("vector", lambda E: E.tensor_scalar(
            srow[:, 0:1], srow[:, 0:1], 1e-5, None, ALU.max), sem="vec")
        p.wait("tensor", "vec", p.cnt["vec"])
        p.op("tensor", lambda E: E.matmul(
            PS2[:, 0:1], C["ones1"], srow[:, 0:1], start=True, stop=True), sem="mm")
        p.wait("scalar", "mm", p.cnt["mm"])
        p.op("scalar", lambda E: E.copy(C["alpha"][:, col:col + 1], PS2[:, 0:1]), sem="scl")
        p.wait("vector", "scl", p.cnt["scl"])
        p.op("vector", lambda E: E.reciprocal(
            C["alpha"][:, col + 1:col + 2], C["alpha"][:, col:col + 1]), sem="vec")
        last_q = [0, 0]
        ldm2 = [0] * (nkk + 1)

        def prep_load2(c):
            pr = c % 2
            if last_q[pr]:
                p.wait("sync", "vec", last_q[pr])
            k, sub = divmod(c, nsub)
            ldm2[c] = p.dma("sync", "ld%d" % pr, stage_f[pr],
                            wv[:, k, sub * SW:(sub + 1) * SW])

        prep_load2(0)
        for c in range(nkk):
            pr = c % 2
            k, sub = divmod(c, nsub)
            p.wait("vector", "ld%d" % pr, ldm2[c])
            if c + 1 < nkk:
                prep_load2(c + 1)
            p.op("vector", lambda E, pr=pr: E.tensor_scalar(
                stage_f[pr], stage_f[pr], C["alpha"][:, col + 1:col + 2], MAGIC,
                ALU.mult, ALU.add), sem="vec")
            p.op("vector", lambda E, pr=pr: E.tensor_scalar(
                stage_f[pr], stage_f[pr], MAGIC, -1.0, ALU.subtract, ALU.max), sem="vec")
            last_q[pr] = p.op("vector", lambda E, k=k, sub=sub, pr=pr: E.tensor_scalar(
                wq[:, k, sub * SW:(sub + 1) * SW], stage_f[pr], 1.0, None,
                ALU.min), sem="vec")

    wq_up = ARB(0, KC, I)
    F0 = ARB(65536, KC, TL)
    F1 = ARB(65536 + 4096, KC, TL)
    Qb = [ARB(73728, KC, TL), ARB(77824, KC, TL)]
    B0 = ARB(81920, IC, TL)
    B0f = arb_t[:, 81920:90112]
    vqT = {("f", 0): ARF(0, 4, TL), ("f", 1): ARF(1024, 4, TL),
           ("b", 0): ARF(2048, 4, TL), ("b", 1): ARF(3072, 4, TL)}
    hrc = [ARF(4096 + i * 512, 1, 512) for i in range(2)]
    fA = ARF(5120, 1, 2)
    fB = ARF(5122, 1, 2)
    fC = ARF(5124, 1, 2)
    fD = ARF(5126, 1, 2)
    fE = ARF(5128, 1, 2)
    fF = ARF(5130, 1, 2)
    tokG2 = [ARF(5132, 1, 2), ARF(5134, 1, 2)]
    bA = ARF(5136, 1, 2)
    bB = ARF(5138, 1, 2)
    bC_ = ARF(5140, 1, 2)
    bD = ARF(5142, 1, 2)
    bE = ARF(5144, 1, 2)
    bH = ARF(5146, 1, 2)
    srowF = ARFR(5152, 256)
    srowB = ARFR(5408, 256)
    csbcF = ARF(5664, 1, TL)
    cs2bcB = ARF(5920, 1, TL)
    amF32 = ARF(6176, 1, TL)
    amB32 = ARF(6432, 1, TL)

    PF = [PBIG[:, 0:512], PBIG[:, 512:1024]]
    PU = [PBIG[:, 1024:1536], PBIG[:, 1536:2048]]

    p.wait("gpsimd", "mm", p.cnt["mm"])
    p.wait("sync", "gxh")
    p.wait("sync", "sp")
    ternary_prep(w_upT, wq_up, KC, I, 0)

    p.sem("gps")
    m = dict(F0_mm=p.cnt["mm"], F1_v=p.cnt["vec"], F1_g=p.cnt["gps"],
             Qv=[p.cnt["vec"]] * 2, Qg=[p.cnt["gps"]] * 2,
             upfree=[p.cnt["mm"]] * 2, B0_v=p.cnt["vec"], B0_g=p.cnt["gps"],
             ps2F_v=p.cnt["vec"], csbcF_scl=p.cnt["scl"],
             ps2B_v=p.cnt["vec"], cs2b_scl=p.cnt["scl"],
             sq_scl=p.cnt["scl"], xh2_scl=p.cnt["scl"], TB_mm=p.cnt["mm"],
             TF_mm=p.cnt["mm"], upmm_done=p.cnt["mm"],
             Qg_free=[p.cnt["gps"]] * 2)
    pf_evac = [p.cnt["scl"], p.cnt["scl"]]
    pu_evac = [p.cnt["scl"], p.cnt["scl"]]
    ldC = [0] * (NT + 1)

    def c_bfly(src, dst, nch, h, first_waits=None):
        n = nch // (2 * h)
        sv = src.rearrange("p (n t h) w -> p n t h w", t=2, h=h)
        dv = dst.rearrange("p (n t h) w -> p n t h w", t=2, h=h)
        if first_waits:
            for eng, sem, val in first_waits:
                p.wait(eng, sem, val)
        p.op("vector", lambda E: E.tensor_tensor(
            dv[:, :, 0], sv[:, :, 0], sv[:, :, 1], ALU.add), sem="vec")
        p.op("gpsimd", lambda E: E.tensor_tensor(
            dv[:, :, 1], sv[:, :, 0], sv[:, :, 1], ALU.subtract), sem="gps")
        return p.cnt["vec"], p.cnt["gps"]

    def c_fwht(bufs, nch, first_waits):
        h = nch // 2
        vm = gm = None
        i = 0
        fw = first_waits
        while h >= 1:
            src, dst = bufs[i % 2], bufs[(i + 1) % 2]
            vm, gm = c_bfly(src, dst, nch, h, fw)
            fw = [("vector", "gps", gm), ("gpsimd", "vec", vm)]
            h //= 2
            i += 1
        return vm, gm

    def c_quant(kind, ng, src3, bcbuf, dst3, pre_waits):
        clip_mark = {}
        for g in range(ng):
            tmp = vqT[(kind, g % 2)]
            if g == 0:
                for s_, v_ in pre_waits:
                    p.wait("gpsimd", s_, v_)
                    p.wait("vector", s_, v_)
            if g >= 2:
                p.wait("gpsimd", "vec", clip_mark[g - 2])
            sl = slice(g * 4, g * 4 + 4)
            p.op("gpsimd", lambda E, sl=sl, tmp=tmp: E.tensor_tensor(
                tmp, src3[:, sl, :], bc3(bcbuf, 4), ALU.mult), sem="gps")
            p.wait("vector", "gps", p.cnt["gps"])
            p.op("vector", lambda E, tmp=tmp: E.tensor_scalar(
                tmp, tmp, MAGIC, MAGIC, ALU.add, ALU.subtract), sem="vec")
            p.op("vector", lambda E, sl=sl, tmp=tmp: E.tensor_scalar(
                dst3[:, sl, :], tmp, -127.0, 127.0, ALU.max, ALU.min), sem="vec")
            clip_mark[g] = p.cnt["vec"]
        return p.cnt["vec"], p.cnt["gps"]

    def c_load(t):
        p.wait("sync", "mm", m["F0_mm"])
        p.wait("sync", "vec", m["F1_v"])
        ldC[t] = p.dma("sync", "ld0", F0, V["xhatb"][:, :, t * TL:(t + 1) * TL])

    def c_front_pre(t):
        if debug:
            p.wait("vector", "sp")
            p.wait("gpsimd", "sp")
        fw = [("vector", "ld0", ldC[t]), ("vector", "mm", m["F0_mm"]),
              ("vector", "gps", m["F1_g"]),
              ("gpsimd", "ld0", ldC[t]), ("gpsimd", "mm", m["F0_mm"]),
              ("gpsimd", "vec", m["F1_v"])]
        return c_fwht([F0, F1], KC, fw)

    def c_h128(src3, nch, dst3view, vm, gm, f1_wait=False):
        slots = PF if nch == KC else PU
        evac = pf_evac if nch == KC else pu_evac
        p.wait("tensor", "vec", vm)
        p.wait("tensor", "gps", gm)
        for j in range(nch // 2):
            slot = j % 2
            p.wait("tensor", "scl", evac[slot])
            p.op("tensor", lambda E, j=j, slot=slot: E.matmul(
                slots[slot][:, 0:TL], C["H"], src3[:, 2 * j, :],
                start=True, stop=True))
            p.op("tensor", lambda E, j=j, slot=slot: E.matmul(
                slots[slot][:, TL:2 * TL], C["H"], src3[:, 2 * j + 1, :],
                start=True, stop=True), sem="mm")
            gmk = p.cnt["mm"]
            p.wait("scalar", "mm", gmk)
            if j == 0 and f1_wait:
                p.wait("scalar", "gps", m["F1_g"])
                p.wait("scalar", "vec", m["F1_v"])
            p.op("scalar", lambda E, j=j, slot=slot: E.copy(
                dst3view[:, 2 * j:2 * j + 2, :].rearrange("p c w -> p (c w)"),
                slots[slot]), sem="scl")
            evac[slot] = p.cnt["scl"]
        return p.cnt["mm"], p.cnt["scl"]

    def c_front_tail(t):
        par = t % 2
        h_mm, h_scl = c_h128(F0, KC, F1, m["bf1_v"], m["bf1_g"], f1_wait=True)
        m["F0_mm"] = h_mm
        p.wait("vector", "scl", h_scl)
        p.wait("vector", "mm", h_mm)
        p.wait("vector", "mm", m["TF_mm"])
        p.op("vector", lambda E: E.tensor_tensor(
            F0[:, 0:8], F1[:, 0:8], F1[:, 8:16], ALU.max), sem="vec")
        p.op("vector", lambda E: E.tensor_tensor(
            F0[:, 8:16], F1[:, 0:8], F1[:, 8:16], ALU.min), sem="vec")
        p.wait("vector", "vec", p.cnt["vec"])
        p.op("vector", lambda E: E.tensor_tensor(
            F0[:, 0:4], F0[:, 0:4], F0[:, 4:8], ALU.max), sem="vec")
        p.op("vector", lambda E: E.tensor_tensor(
            F0[:, 8:12], F0[:, 8:12], F0[:, 12:16], ALU.min), sem="vec")
        p.wait("vector", "vec", p.cnt["vec"])
        p.op("vector", lambda E: E.tensor_tensor(
            F0[:, 0:2], F0[:, 0:2], F0[:, 2:4], ALU.max), sem="vec")
        p.op("vector", lambda E: E.tensor_tensor(
            F0[:, 8:10], F0[:, 8:10], F0[:, 10:12], ALU.min), sem="vec")
        p.wait("vector", "vec", p.cnt["vec"])
        p.op("vector", lambda E: E.tensor_tensor(
            F0[:, 0:1], F0[:, 0:1], F0[:, 1:2], ALU.max), sem="vec")
        p.op("vector", lambda E: E.tensor_tensor(
            F0[:, 8:9], F0[:, 8:9], F0[:, 9:10], ALU.min), sem="vec")
        p.wait("vector", "vec", p.cnt["vec"])
        p.op("vector", lambda E: E.scalar_tensor_tensor(
            amF32, F0[:, 8, :], -1.0, F0[:, 0, :], ALU.mult, ALU.max), sem="vec")
        tree_v = p.cnt["vec"]
        if debug:
            p.wait("sync", "scl", h_scl)
            p.dma("sync", "sp", xh1_dbg_v[:, :, t * TL:(t + 1) * TL], F1)
            p.wait("sync", "vec", tree_v)
            p.dma("sync", "sp", am_dbg[t], amF32)
        p.wait("tensor", "vec", tree_v)
        p.op("tensor", lambda E: E.transpose(PS2[:, 0:128], amF32[:, 0:128], C["idf"]))
        p.op("tensor", lambda E: E.transpose(PS2[:, 128:256], amF32[:, 128:256], C["idf"]),
             sem="mm")
        m["TF_mm"] = p.cnt["mm"]
        p.wait("vector", "mm", m["TF_mm"])
        if debug:
            p.wait("vector", "sp")
        p.op("vector", lambda E: E.tensor_reduce(
            fA[:, 0:1], PS2[:, 0:128], AX.X, ALU.max), sem="vec")
        p.op("vector", lambda E: E.tensor_reduce(
            fA[:, 1:2], PS2[:, 128:256], AX.X, ALU.max), sem="vec")
        m["ps2F_v"] = p.cnt["vec"]
        p.wait("vector", "vec", p.cnt["vec"])
        p.op("vector", lambda E: E.tensor_scalar(
            fB, fA, C_D, 1e-5, ALU.mult, ALU.max), sem="vec")
        p.wait("vector", "vec", p.cnt["vec"])
        p.op("vector", lambda E: E.tensor_scalar(
            fC, fB, 1.0 / 127.0, None, ALU.mult), sem="vec")
        p.wait("vector", "vec", p.cnt["vec"])
        p.op("vector", lambda E: E.reciprocal(fD, fC), sem="vec")
        p.wait("vector", "vec", p.cnt["vec"])
        p.op("vector", lambda E: E.tensor_scalar(
            fE, fD, C_D, None, ALU.mult), sem="vec")
        p.wait("vector", "vec", p.cnt["vec"])
        p.op("vector", lambda E: E.tensor_scalar(
            fF, fC, C["alpha"][:, 0:1], None, ALU.mult), sem="vec")
        p.wait("vector", "vec", p.cnt["vec"])
        p.op("vector", lambda E, par=par: E.tensor_tensor(
            tokG2[par], fF, fF, ALU.mult), sem="vec")
        chain_v = p.cnt["vec"]
        if debug:
            p.wait("sync", "vec", chain_v)
            p.dma("sync", "sp", tok_dbg[t, :, 0:2], fA)
            p.dma("sync", "sp", tok_dbg[t, :, 2:4], fE)
            p.dma("sync", "sp", tok_dbg[t, :, 4:6], fC)
        p.wait("tensor", "vec", chain_v)
        p.wait("tensor", "scl", m["csbcF_scl"])
        p.op("tensor", lambda E: E.transpose(PS2[0:1, 256:384], fE[:, 0:1], C["idf"]))
        p.op("tensor", lambda E: E.transpose(PS2[0:1, 384:512], fE[:, 1:2], C["idf"]),
             sem="mm")
        p.wait("scalar", "mm", p.cnt["mm"])
        p.op("scalar", lambda E: E.copy(srowF[0:1, :], PS2[0:1, 256:512]), sem="scl")
        if debug:
            p.wait("sync", "scl", p.cnt["scl"])
            p.dma("sync", "sp", srow_dbg[t], srowF[0:1, :])
        p.wait("tensor", "scl", p.cnt["scl"])
        p.op("tensor", lambda E: E.matmul(
            PS2[:, 256:384], C["ones1"], srowF[0:1, 0:128], start=True, stop=True))
        p.op("tensor", lambda E: E.matmul(
            PS2[:, 384:512], C["ones1"], srowF[0:1, 128:256], start=True, stop=True),
            sem="mm")
        p.wait("scalar", "mm", p.cnt["mm"])
        p.op("scalar", lambda E: E.copy(csbcF, PS2[:, 256:512]), sem="scl")
        m["csbcF_scl"] = p.cnt["scl"]
        pre = [("scl", m["csbcF_scl"]), ("mm", m["upfree"][par]),
               ("gps", m["Qg_free"][par])]
        vv, gg = c_quant("f", 4, F1, csbcF, Qb[par], pre)
        if debug:
            p.wait("sync", "vec", vv)
            p.dma("sync", "sp", q1_dbg_v[:, :, t * TL:(t + 1) * TL], Qb[par])
            p.wait("sync", "scl", m["csbcF_scl"])
            p.dma("sync", "sp", cs_dbg[t, :, 0:256], csbcF)
        m["F1_v"] = tree_v
        m["F1_g"] = gg
        m["Qv"][par] = vv
        m["Qg"][par] = gg

    def c_up(t, g0, g1):
        par = t % 2
        for g in range(g0, g1):
            slot = g % 2
            p.wait("tensor", "scl", pu_evac[slot])
            if g == 0:
                p.wait("tensor", "vec", m["Qv"][par])
                p.wait("tensor", "gps", m["Qg"][par])
            for jj in range(2):
                oc = 2 * g + jj
                for k in range(KC):
                    last = (jj == 1) and (k == KC - 1)
                    p.op("tensor", lambda E, k=k, oc=oc, jj=jj, slot=slot, par=par: E.matmul(
                        PU[slot][:, jj * TL:(jj + 1) * TL],
                        wq_up[:, k, oc * 128:(oc + 1) * 128], Qb[par][:, k, :],
                        start=(k == 0), stop=(k == KC - 1)),
                        sem="mm" if last else None)
            gmk = p.cnt["mm"]
            p.wait("scalar", "mm", gmk)
            if g == 0:
                p.wait("scalar", "gps", m["B0_g"])
                p.wait("scalar", "vec", m["B0_v"])
                p.wait("scalar", "sp")
            p.op("scalar", lambda E, g=g, slot=slot: E.activation(
                hrc[g % 2], PU[slot], AF.Relu, bias=np.float32(0.0),
                scale=np.float32(1.0)), sem="scl")
            pu_evac[slot] = p.cnt["scl"]
            p.op("scalar", lambda E, g=g: E.square(
                B0f[:, 2 * g * TL:(2 * g + 2) * TL], hrc[g % 2]), sem="scl")
        if g1 == IC // 2:
            m["sq_scl"] = p.cnt["scl"]
            m["upmm_done"] = p.cnt["mm"]
            if debug:
                p.wait("sync", "scl", m["sq_scl"])
                p.dma("sync", "sp", h_dbg_v[:, :, t * TL:(t + 1) * TL], B0)

    def c_back_tail(t):
        par = t % 2
        Qv = Qb[par]
        B0lo = B0[:, 0:16, :]
        B0hi = B0[:, 16:32, :]
        p.wait("vector", "scl", m["sq_scl"])
        p.wait("vector", "sp")
        p.wait("vector", "mm", m["upmm_done"])
        p.op("vector", lambda E: E.tensor_tensor(Qv, B0lo, B0hi, ALU.add), sem="vec")
        s1v = p.cnt["vec"]
        p.wait("gpsimd", "vec", s1v)
        p.wait("gpsimd", "scl", m["sq_scl"])
        p.wait("gpsimd", "sp")
        p.op("gpsimd", lambda E: E.tensor_tensor(
            B0hi, B0lo, B0hi, ALU.subtract), sem="gps")
        s1g = p.cnt["gps"]
        fwA = [("vector", "gps", s1g), ("gpsimd", "vec", s1v)]
        vmA, gmA = c_fwht([Qv, B0lo], KC, fwA)
        fwB = [("vector", "gps", gmA), ("gpsimd", "vec", vmA)]
        vm, gm = c_fwht([B0hi, B0lo], KC, fwB)
        p.wait("tensor", "vec", vm)
        p.wait("tensor", "gps", gm)
        for j in range(IC // 2):
            slot = j % 2
            p.wait("tensor", "scl", pu_evac[slot])
            src = Qv if j < 8 else B0hi
            c0 = (2 * j) % 16
            p.op("tensor", lambda E, src=src, c0=c0, slot=slot: E.matmul(
                PU[slot][:, 0:TL], C["H"], src[:, c0, :], start=True, stop=True))
            p.op("tensor", lambda E, src=src, c0=c0, slot=slot: E.matmul(
                PU[slot][:, TL:2 * TL], C["H"], src[:, c0 + 1, :],
                start=True, stop=True), sem="mm")
            gmk = p.cnt["mm"]
            p.wait("scalar", "mm", gmk)
            p.op("scalar", lambda E, j=j, slot=slot: E.copy(
                B0[:, 2 * j:2 * j + 2, :].rearrange("p c w -> p (c w)"),
                PU[slot]), sem="scl")
            pu_evac[slot] = p.cnt["scl"]
        h_mm = p.cnt["mm"]
        m["xh2_scl"] = p.cnt["scl"]
        m["upfree"][par] = h_mm
        m["Qg_free"][par] = gmA
        p.wait("vector", "scl", m["xh2_scl"])
        p.wait("vector", "mm", h_mm)
        p.wait("vector", "mm", m["TB_mm"])
        p.op("vector", lambda E: E.tensor_tensor(
            Qv[:, 0:8], B0[:, 0:8], B0[:, 8:16], ALU.max), sem="vec")
        p.op("vector", lambda E: E.tensor_tensor(
            Qv[:, 8:16], B0[:, 0:8], B0[:, 8:16], ALU.min), sem="vec")
        p.wait("vector", "vec", p.cnt["vec"])
        p.op("vector", lambda E: E.tensor_tensor(
            Qv[:, 0:8], Qv[:, 0:8], B0[:, 16:24], ALU.max), sem="vec")
        p.op("vector", lambda E: E.tensor_tensor(
            Qv[:, 8:16], Qv[:, 8:16], B0[:, 16:24], ALU.min), sem="vec")
        p.wait("vector", "vec", p.cnt["vec"])
        p.op("vector", lambda E: E.tensor_tensor(
            Qv[:, 0:8], Qv[:, 0:8], B0[:, 24:32], ALU.max), sem="vec")
        p.op("vector", lambda E: E.tensor_tensor(
            Qv[:, 8:16], Qv[:, 8:16], B0[:, 24:32], ALU.min), sem="vec")
        p.wait("vector", "vec", p.cnt["vec"])
        p.op("vector", lambda E: E.tensor_tensor(
            Qv[:, 0:4], Qv[:, 0:4], Qv[:, 4:8], ALU.max), sem="vec")
        p.op("vector", lambda E: E.tensor_tensor(
            Qv[:, 8:12], Qv[:, 8:12], Qv[:, 12:16], ALU.min), sem="vec")
        p.wait("vector", "vec", p.cnt["vec"])
        p.op("vector", lambda E: E.tensor_tensor(
            Qv[:, 0:2], Qv[:, 0:2], Qv[:, 2:4], ALU.max), sem="vec")
        p.op("vector", lambda E: E.tensor_tensor(
            Qv[:, 8:10], Qv[:, 8:10], Qv[:, 10:12], ALU.min), sem="vec")
        p.wait("vector", "vec", p.cnt["vec"])
        p.op("vector", lambda E: E.tensor_tensor(
            Qv[:, 0:1], Qv[:, 0:1], Qv[:, 1:2], ALU.max), sem="vec")
        p.op("vector", lambda E: E.tensor_tensor(
            Qv[:, 8:9], Qv[:, 8:9], Qv[:, 9:10], ALU.min), sem="vec")
        p.wait("vector", "vec", p.cnt["vec"])
        p.op("vector", lambda E: E.scalar_tensor_tensor(
            amB32, Qv[:, 8, :], -1.0, Qv[:, 0, :], ALU.mult, ALU.max), sem="vec")
        tree_v = p.cnt["vec"]
        p.wait("tensor", "vec", tree_v)
        p.op("tensor", lambda E: E.transpose(PS2[:, 512:640], amB32[:, 0:128], C["idf"]))
        p.op("tensor", lambda E: E.transpose(PS2[:, 640:768], amB32[:, 128:256], C["idf"]),
             sem="mm")
        m["TB_mm"] = p.cnt["mm"]
        p.wait("vector", "mm", m["TB_mm"])
        p.op("vector", lambda E: E.tensor_reduce(
            bA[:, 0:1], PS2[:, 512:640], AX.X, ALU.max), sem="vec")
        p.op("vector", lambda E: E.tensor_reduce(
            bA[:, 1:2], PS2[:, 640:768], AX.X, ALU.max), sem="vec")
        m["ps2B_v"] = p.cnt["vec"]
        p.wait("vector", "vec", p.cnt["vec"])
        p.wait("vector", "sp")
        p.op("vector", lambda E: E.tensor_scalar(
            bB, bA, C_I, 1e-5, ALU.mult, ALU.max), sem="vec")
        p.wait("vector", "vec", p.cnt["vec"])
        p.op("vector", lambda E: E.tensor_scalar(
            bC_, bB, 1.0 / 127.0, None, ALU.mult), sem="vec")
        p.wait("vector", "vec", p.cnt["vec"])
        p.op("vector", lambda E: E.reciprocal(bD, bC_), sem="vec")
        p.wait("vector", "vec", p.cnt["vec"])
        p.op("vector", lambda E: E.tensor_scalar(
            bE, bD, C_I, None, ALU.mult), sem="vec")
        p.wait("vector", "vec", p.cnt["vec"])
        p.op("vector", lambda E, par=par: E.tensor_tensor(
            bH, tokG2[par], bC_, ALU.mult), sem="vec")
        chain_v = p.cnt["vec"]
        p.wait("sync", "vec", chain_v)
        p.dma("sync", "sp", fold_sp[t], bH)
        p.wait("tensor", "vec", chain_v)
        p.wait("tensor", "scl", m["cs2b_scl"])
        p.op("tensor", lambda E: E.transpose(PS2[0:1, 768:896], bE[:, 0:1], C["idf"]))
        p.op("tensor", lambda E: E.transpose(PS2[0:1, 896:1024], bE[:, 1:2], C["idf"]),
             sem="mm")
        p.wait("scalar", "mm", p.cnt["mm"])
        p.op("scalar", lambda E: E.copy(srowB[0:1, :], PS2[0:1, 768:1024]), sem="scl")
        p.wait("tensor", "scl", p.cnt["scl"])
        p.op("tensor", lambda E: E.matmul(
            PS2[:, 768:896], C["ones1"], srowB[0:1, 0:128], start=True, stop=True))
        p.op("tensor", lambda E: E.matmul(
            PS2[:, 896:1024], C["ones1"], srowB[0:1, 128:256], start=True, stop=True),
            sem="mm")
        p.wait("scalar", "mm", p.cnt["mm"])
        p.op("scalar", lambda E: E.copy(cs2bcB, PS2[:, 768:1024]), sem="scl")
        m["cs2b_scl"] = p.cnt["scl"]
        if debug:
            p.wait("sync", "scl", m["xh2_scl"])
            p.dma("sync", "sp", xh2_dbg_v[:, :, t * TL:(t + 1) * TL], B0)
            p.wait("sync", "scl", m["cs2b_scl"])
            p.dma("sync", "sp", cs_dbg[t, :, 256:512], cs2bcB)
            p.wait("vector", "sp")
            p.wait("gpsimd", "sp")
        pre = [("scl", m["cs2b_scl"]), ("mm", m["TB_mm"])]
        vv, gg = c_quant("b", 8, B0, cs2bcB, B0, pre)
        m["B0_v"] = vv
        m["B0_g"] = gg
        p.wait("sync", "vec", vv)
        p.wait("sync", "gps", gg)
        p.dma("sync", "sp", V["xq2"][:, :, t * TL:(t + 1) * TL], B0)

    c_load(0)
    m["bf1_v"], m["bf1_g"] = c_front_pre(0)
    c_front_tail(0)
    for t in range(NT):
        if t + 1 < NT:
            c_load(t + 1)
        c_up(t, 0, 8)
        if t + 1 < NT:
            nv, ng = c_front_pre(t + 1)
        c_up(t, 8, IC // 2)
        if t + 1 < NT:
            m["bf1_v"], m["bf1_g"] = nv, ng
            c_front_tail(t + 1)
        c_back_tail(t)

    wq_dn = ARB(0, IC, D)
    xq2t2 = [ARB(65536, IC, TL), ARB(65536 + 8192, IC, TL)]
    rsD = ARB(65536 + 16384, KC, TL)
    xht = ARF(0, KC, TL)
    rowsD = ARFR(5120, 4 * TL)
    mrowD = ARFR(6144, TL)
    foldt = ARF(6144, 1, 2)
    sct = ARF(6146, 1, 2)
    bcD = ARF(6400, 1, 2 * TL)
    srowD = ARFR(6912, 256)
    outT = ARF(4096, 1, 1024)
    PS2c4 = PS2[:, 0:1024].rearrange("p (c w) -> p c w", c=4)
    out_mark = 0
    ocopy_mark = 0

    p.wait("gpsimd", "mm", p.cnt["mm"])
    p.wait("sync", "sp")
    p.wait("sync", "vec", p.cnt["vec"])
    p.wait("sync", "gps", p.cnt["gps"])
    p.wait("scalar", "vec", p.cnt["vec"])
    p.wait("scalar", "gps", p.cnt["gps"])
    ternary_prep(w_dnT, wq_dn, IC, D, 2)
    p.wait("sync", "mm", p.cnt["mm"])
    if debug:
        p.wait("sync", "vec", p.cnt["vec"])
        p.dma("sync", "sp", wqd_dbg, wq_dn)
        p.dma("sync", "sp", alpha_dbg, C["alpha"])

    free_reg = [0, 0]
    free_dnc = 0
    free_xht_mm = 0
    free_xq2 = [0, 0]
    ldD_mark = [0] * (NT + 1)
    ldD2_mark = [0] * (NT + 1)

    def d_load(t):
        par = t % 2
        if free_xq2[par]:
            p.wait("sync", "mm", free_xq2[par])
        ldD_mark[t] = p.dma("sync", "ld%d" % par, xq2t2[par],
                            V["xq2"][:, :, t * TL:(t + 1) * TL])

    d_load(0)
    for t in range(NT):
        par = t % 2
        xq2t = xq2t2[par]
        if free_xht_mm:
            p.wait("sync", "mm", free_xht_mm)
        p.dma("sync", "sp", xht, V["xhat"][:, :, t * TL:(t + 1) * TL])
        ldD2_mark[t] = p.dma("sync", "sp", foldt, fold_sp[t])
        p.wait("vector", "sp", ldD2_mark[t])
        p.op("vector", lambda E: E.tensor_scalar(
            sct, foldt, C["alpha"][:, 2:3], None, ALU.mult), sem="vec")
        sct_vec = p.cnt["vec"]
        p.wait("tensor", "ld%d" % par, ldD_mark[t])
        for g in range(4):
            reg = g % 2
            if free_reg[reg]:
                p.wait("tensor", "vec", free_reg[reg])
            for oi in range(4):
                oc = g * 4 + oi
                for k in range(IC):
                    last = (k == IC - 1) and (oi == 3)
                    p.op("tensor", lambda E, k=k, oc=oc, oi=oi, reg=reg, x=xq2t: E.matmul(
                        PBIG[:, reg * 1024 + oi * TL: reg * 1024 + (oi + 1) * TL],
                        wq_dn[:, k, oc * 128:(oc + 1) * 128], x[:, k, :],
                        start=(k == 0), stop=(k == IC - 1)),
                        sem="mm" if last else None)
            if g == 0:
                p.wait("tensor", "vec", sct_vec)
                p.wait("tensor", "scl", p.cnt["scl"])
                p.op("tensor", lambda E: E.transpose(PS2[0:1, 256:384], sct[:, 0:1], C["idf"]))
                p.op("tensor", lambda E: E.transpose(PS2[0:1, 384:512], sct[:, 1:2], C["idf"]), sem="mm")
                p.wait("scalar", "mm", p.cnt["mm"])
                p.op("scalar", lambda E: E.copy(srowD[0:1, :], PS2[0:1, 256:512]), sem="scl")
                p.wait("tensor", "scl", p.cnt["scl"])
                p.op("tensor", lambda E: E.matmul(
                    PS2[:, 512:640], C["ones1"], srowD[0:1, 0:128], start=True, stop=True))
                p.op("tensor", lambda E: E.matmul(
                    PS2[:, 640:768], C["ones1"], srowD[0:1, 128:256], start=True, stop=True),
                    sem="mm")
                p.wait("scalar", "mm", p.cnt["mm"])
                p.op("scalar", lambda E: E.copy(bcD[:, 0:TL], PS2[:, 512:768]), sem="scl")
                bc_scl = p.cnt["scl"]
            p.wait("vector", "mm", p.cnt["mm"])
            if g == 0:
                p.wait("vector", "scl", bc_scl)
                p.wait("vector", "sp")
                if ocopy_mark:
                    p.wait("vector", "scl", ocopy_mark)
            p.op("vector", lambda E, reg=reg: E.tensor_tensor(
                PS2c4, PBIG[:, reg * 1024:reg * 1024 + 1024]
                .rearrange("p (c w) -> p c w", c=4),
                bc3(bcD[:, 0:TL], 4), ALU.mult), sem="vec")
            p.op("vector", lambda E, g=g: E.tensor_tensor(
                xht[:, 4 * g:4 * g + 4, :], xht[:, 4 * g:4 * g + 4, :],
                PS2c4, ALU.add), sem="vec")
            free_reg[reg] = p.cnt["vec"]
        free_xq2[par] = p.cnt["mm"]
        if t + 1 < NT:
            d_load(t + 1)
        p.wait("scalar", "vec", p.cnt["vec"])
        p.op("scalar", lambda E: E.copy(rsD[:], xht[:]), sem="scl")
        p.wait("tensor", "scl", p.cnt["scl"])
        for k in range(KC):
            last = k == KC - 1
            p.op("tensor", lambda E, k=k: E.matmul(
                PS_ROW[:, 0:TL], C["onesc"], rsD[:, k, :],
                start=(k == 0), stop=last), sem="mm" if last else None)
        p.wait("scalar", "mm", p.cnt["mm"])
        p.op("scalar", lambda E: E.square(rsD[:], xht[:]), sem="scl")
        p.wait("tensor", "scl", p.cnt["scl"])
        for k in range(KC):
            last = k == KC - 1
            p.op("tensor", lambda E, k=k: E.matmul(
                PS_ROW[:, 512:512 + TL], C["onesc"], rsD[:, k, :],
                start=(k == 0), stop=last), sem="mm" if last else None)
        p.wait("vector", "mm", p.cnt["mm"])
        ln_rows(rowsD, mrowD, PS_ROW[:, 0:TL], PS_ROW[:, 512:512 + TL])
        p.wait("tensor", "vec", p.cnt["vec"])
        p.op("tensor", lambda E: E.matmul(
            PS2[:, 0:TL], C["ones1"], rowsD[:, 3 * TL:4 * TL], start=True, stop=True))
        p.op("tensor", lambda E: E.matmul(
            PS2[:, 512:512 + TL], C["ones1"], rowsD[:, 0:TL], start=True, stop=True),
            sem="mm")
        p.wait("scalar", "mm", p.cnt["mm"])
        p.op("scalar", lambda E: E.copy(bcD[:, 0:TL], PS2[:, 0:TL]), sem="scl")
        p.op("scalar", lambda E: E.copy(bcD[:, TL:2 * TL], PS2[:, 512:512 + TL]), sem="scl")
        p.wait("vector", "scl", p.cnt["scl"])
        p.op("vector", lambda E: E.tensor_tensor(
            xht[:], xht[:], bc3(bcD[:, 0:TL], KC), ALU.mult), sem="vec")
        p.op("vector", lambda E: E.tensor_tensor(
            xht[:], xht[:], bc3(bcD[:, TL:2 * TL], KC), ALU.subtract), sem="vec")
        p.wait("tensor", "vec", p.cnt["vec"])
        p.wait("tensor", "scl", p.cnt["scl"])
        for tc in range(2):
            for half in range(2):
                if ocopy_mark:
                    p.wait("tensor", "scl", ocopy_mark)
                for e8 in range(8):
                    e = half * 8 + e8
                    p.op("tensor", lambda E, e=e, e8=e8, tc=tc: E.transpose(
                        PS2[:, e8 * 128:(e8 + 1) * 128],
                        xht[:, e, tc * 128:(tc + 1) * 128], C["idf"]),
                        sem="mm" if e8 == 7 else None)
                p.wait("scalar", "mm", p.cnt["mm"])
                if out_mark:
                    p.wait("scalar", "out", out_mark)
                p.op("scalar", lambda E: E.copy(outT, PS2[:, 0:1024]), sem="scl")
                ocopy_mark = p.cnt["scl"]
                p.wait("sync", "scl", ocopy_mark)
                out_mark = p.dma(
                    "sync", "out",
                    y_out[t * TL + tc * 128:t * TL + (tc + 1) * 128,
                          half * 1024:(half + 1) * 1024], outT)
        free_xht_mm = p.cnt["mm"]

    p.wait("sync", "out")

    with p.es:
        with nc.Block() as block:
            @block.sync
            def _(E):
                for f in p.ops["sync"]:
                    f(E)

            @block.gpsimd
            def _(E):
                for f in p.ops["gpsimd"]:
                    f(E)

            @block.tensor
            def _(E):
                for f in p.ops["tensor"]:
                    f(E)

            @block.vector
            def _(E):
                for f in p.ops["vector"]:
                    f(E)

            @block.scalar
            def _(E):
                for f in p.ops["scalar"]:
                    f(E)
    return nc


def kernel(**inputs):
    x = np.ascontiguousarray(np.asarray(inputs["L0_attn_x"], np.float32))
    B, S, d = x.shape
    assert (B, S, d) == (2, 16384, 2048)

    def prep_bias(v):
        return np.ascontiguousarray(np.asarray(v, np.float32).reshape(KC, 128).T)

    H = np.array([[1.0]])
    while H.shape[0] < 128:
        H = np.block([[H, H], [H, -H]])
    common = dict(
        W_in=np.ascontiguousarray(np.asarray(inputs["L0_attn_W_in"], np.float32)),
        W_dec=np.ascontiguousarray(np.asarray(inputs["L0_attn_W_decay"], np.float32)),
        W_gate=np.ascontiguousarray(np.asarray(inputs["L0_attn_W_gate"], np.float32)),
        W_out=np.ascontiguousarray(np.asarray(inputs["L0_attn_W_out"], np.float32)),
        w_upT=np.ascontiguousarray(np.asarray(inputs["L0_ffn_w_up"], np.float32).T),
        w_dnT=np.ascontiguousarray(np.asarray(inputs["L0_ffn_w_down"], np.float32).T),
        b_in=prep_bias(inputs["L0_attn_b_in"]),
        b_dec=prep_bias(inputs["L0_attn_b_decay"]),
        b_dbias=prep_bias(inputs["L0_attn_decay_bias"]),
        b_gate=prep_bias(inputs["L0_attn_b_gate"]),
        b_out=prep_bias(inputs["L0_attn_b_out"]),
        H128=H.astype(ml_dtypes.bfloat16),
        identb=np.eye(128).astype(ml_dtypes.bfloat16),
        identf=np.eye(128, dtype=np.float32),
    )
    in_maps = []
    for k in range(NCORES):
        b_, q_ = divmod(k, 4)
        sel = np.zeros((128, NCORES), np.float32)
        sel[:, k] = 1.0
        in_maps.append(dict(
            x=np.ascontiguousarray(x[b_, q_ * T:(q_ + 1) * T, :].T),
            sel=sel, **common))

    nc = build_nc()
    res = run_bass_kernel_spmd(nc, in_maps, core_ids=list(range(NCORES)))
    out = np.empty((B, S, d), np.float32)
    for k in range(NCORES):
        b_, q_ = divmod(k, 4)
        out[b_, q_ * T:(q_ + 1) * T, :] = res.results[k]["y"]
    return out
